# version 11
# baseline (speedup 1.0000x reference)
import numpy as np
from contextlib import ExitStack

import concourse.bass as bass
import concourse.mybir as mybir
from concourse.bass_utils import run_bass_kernel_spmd

F32 = mybir.dt.float32
BF16 = mybir.dt.bfloat16
AF = mybir.ActivationFunctionType
ALU = mybir.AluOpType
AX = mybir.AxisListType

D = 1024
H = 2048
E = 8
NG = 8
P = 128


class TView:

    def __init__(self, ap):
        self._ap = ap

    def __getitem__(self, idx):
        return self._ap[idx]

    def ap(self):
        return self._ap


class Bump:
    def __init__(self, blob, total_cols):
        self.blob = blob
        self.off = 0
        self.total = total_cols
        self.hwm = 0

    def take(self, cols):
        cols = (cols + 15) // 16 * 16
        ap = self.blob.ap()[:, self.off:self.off + cols]
        self.off += cols
        self.hwm = max(self.hwm, self.off)
        assert self.off <= self.total, f"arena overflow {self.off} > {self.total}"
        return TView(ap)

    def reset(self):
        self.off = 0


def build(T=4096, TCA=256, TBQ=1024, debug_taps=False, detect_races=True):
    TB = min(TBQ, T)
    NCH = T // TCA
    CC = 2 * TCA
    SS = TCA + 1
    NQ = T // TB
    NB = 2 * NQ
    NSUB = TB // 512
    NTT = TB // 128
    NHC = H // P
    PHA_TOTAL = 3

    nc = bass.Bass("TRN2", target_bir_lowering=False,
                   dynamic_dma_scratch_size=16384,
                   detect_race_conditions=detect_races)

    xT = nc.dram_tensor("xT", [2, D, T], F32, kind="ExternalInput").ap()
    wkT = nc.dram_tensor("wkT", [D, D], F32, kind="ExternalInput").ap()
    wvT = nc.dram_tensor("wvT", [D, D], F32, kind="ExternalInput").ap()
    wrT = nc.dram_tensor("wrT", [D, D], F32, kind="ExternalInput").ap()
    woT = nc.dram_tensor("woT", [D, D], F32, kind="ExternalInput").ap()
    w12Hd = nc.dram_tensor("w12H", [E, H // P, P, 2 * NG * P], F32, kind="ExternalInput").ap()
    w3Hd = nc.dram_tensor("w3H", [E, NHC // 4, P, 4 * NG * P], F32, kind="ExternalInput").ap()
    gwT = nc.dram_tensor("gwT", [D, E], F32, kind="ExternalInput").ap()
    tdec = nc.dram_tensor("tdec", [D, 1], F32, kind="ExternalInput").ap()
    tfir = nc.dram_tensor("tfir", [D, 1], F32, kind="ExternalInput").ap()
    tmk = nc.dram_tensor("tmk", [D, 1], F32, kind="ExternalInput").ap()
    tmv = nc.dram_tensor("tmv", [D, 1], F32, kind="ExternalInput").ap()
    tmr = nc.dram_tensor("tmr", [D, 1], F32, kind="ExternalInput").ap()
    eyeI = nc.dram_tensor("eyeI", [P, P], F32, kind="ExternalInput").ap()
    onesC = nc.dram_tensor("onesC", [P, 1], F32, kind="ExternalInput").ap()
    onesR = nc.dram_tensor("onesR", [1, P], F32, kind="ExternalInput").ap()

    outT = nc.dram_tensor("outT", [2, D, T], F32, kind="ExternalOutput").ap()
    _scrk = "ExternalOutput" if debug_taps else "Internal"
    xn_scr = nc.dram_tensor("xn_scr", [2, D, T], BF16, kind=_scrk).ap()
    p_scr = nc.dram_tensor("p_scr", [E, 2 * TB], F32, kind="Internal").ap()
    s_scr = nc.dram_tensor("s_scr", [2, D, T], F32, kind=_scrk).ap()
    if debug_taps:
        dbgB = nc.dram_tensor("dbgB", [P, NG * TBQ + 3 * TBQ + 64], F32, kind="ExternalOutput").ap()

    es = ExitStack()

    def sb(name, shape, dt):
        return es.enter_context(nc.sbuf_tensor(name, shape, dt))

    def psf(name):
        return es.enter_context(nc.psum_tensor(name, [P, 512], F32))

    w8t = sb("w8t", [P, NG], F32)
    w8 = sb("w8", [P, NG], F32)
    u8 = sb("u8", [P, NG], F32)
    tk8 = sb("tk8", [P, NG], F32)
    tv8 = sb("tv8", [P, NG], F32)
    tr8 = sb("tr8", [P, NG], F32)
    cp = sb("cp", [P, 2 * NG], F32)
    ca = sb("ca", [P, 2 * NG], F32)
    cb = sb("cb", [P, 2 * NG], F32)
    or_sb = sb("or_sb", [1, P], F32)
    oc_sb = sb("oc_sb", [P, 1], F32)
    eye_sb = sb("eye_sb", [P, P], F32)
    rows = sb("rows", [1, 3 * CC], F32)
    gw_sb = sb("gw_sb", [P, E * NG], BF16)

    mrow = rows.ap()[:, 0:CC]
    vrow = rows.ap()[:, CC:2 * CC]
    rstdr = rows.ap()[:, 2 * CC:3 * CC]

    blob32 = sb("blob32", [P, 24688], F32)
    blob16 = sb("blob16", [P, 51200], BF16)
    a32 = Bump(blob32, 24688)
    a16 = Bump(blob16, 51200)

    xbuf = [a32.take(NG * CC), a32.take(NG * CC)]
    xnf = a32.take(NG * 2 * SS)
    kg = [a32.take(CC), a32.take(CC)]
    vg = [a32.take(CC), a32.take(CC)]
    pbuf = [a32.take(2 * SS), a32.take(2 * SS)]
    abuf = [a32.take(2 * SS), a32.take(2 * SS)]
    bbuf = [a32.take(2 * SS), a32.take(2 * SS)]
    _pk = [a32.take(4 * CC), a32.take(4 * CC)]
    f1d = [TView(_pk[p].ap()[:, 0:CC]) for p in range(2)]
    f2d = [TView(_pk[p].ap()[:, CC:2 * CC]) for p in range(2)]
    tmin = [TView(_pk[p].ap()[:, 2 * CC:3 * CC]) for p in range(2)]
    tmax = [TView(_pk[p].ap()[:, 3 * CC:4 * CC]) for p in range(2)]
    erb = [a32.take(CC), a32.take(CC)]
    f2vb = a32.take(CC)
    tA2 = a32.take(CC)
    tB2 = a32.take(CC)
    dbuf = a32.take(CC)
    t0buf = dbuf
    A32_HWM = a32.hwm

    wk_sb = a16.take(NG * D)
    wv_sb = a16.take(NG * D)
    wr_sb = a16.take(NG * D)
    wo_sb = a16.take(NG * D)
    xk_sb = a16.take(NG * CC)
    xv_sb = a16.take(NG * CC)
    xr_sb = a16.take(NG * CC)
    sry = a16.take(NG * CC)
    A16_HWM = a16.hwm

    a32.reset()
    a16.reset()
    gacc = a32.take(NG * TB)
    sxs = [a32.take(TB), a32.take(TB)]
    sxx = [a32.take(TB), a32.take(TB)]
    obuf = a32.take(TB)
    bt1 = a32.take(512)
    P_sb = a32.take(TB)
    gs_sb = a32.take(TB)
    s1t = [a32.take(512), a32.take(512), a32.take(512)]
    scn = a32.take(NTT * E)
    sc2 = a32.take(NTT * E)
    mkb = a32.take(NTT * E)
    w8b = a32.take(NTT * E)
    ewb = a32.take(NTT * E)
    Pn = a32.take(NTT * E)
    m1n8 = a32.take(NTT)
    m28 = a32.take(NTT)
    z8 = a32.take(NTT)
    rz8 = a32.take(NTT)
    PT2 = a32.take(TB)
    PT_sb = TView(a32.take(TB).ap()[0:E, :])
    B32_HWM = a32.hwm

    xnb = a16.take(NG * TB)
    hid = a16.take(NHC * TB)
    w12_sb = [a16.take(2 * NG * P) for _ in range(5)]
    w3_sb = a16.take(NHC * D)
    B16_HWM = a16.hwm

    psA = psf("psA")
    psB = psf("psB")
    psM = psf("psM")
    psR_ = psf("psR_")
    psV0 = psf("psV0")
    psV1 = psf("psV1")
    psR0 = psf("psR0")
    psR1 = psf("psR1")
    psK = [psM, psR_]
    psV = [psV0, psV1]
    psS = [psA, psB]
    psMb = psA
    psRb = psB
    psRr = [psR0, psR1]
    psG = [psA, psB]
    psH1 = [psM, psR_, psV0]
    psH2 = [psV1, psR0, psR1]
    psSc = psA
    psPT = psB

    sems = {}
    for name in ["aux", "w_att", "sc_init", "x_ld", "sq", "stat", "row",
                 "rstd", "bcast", "xnf_done", "mix", "kv", "evr", "d2s", "s2d",
                 "grpdone", "wo", "sev", "ssp", "xnsp", "phA", "phAD", "vraw", "kvcp",
                 "vprod", "gpd",
                 "bxn", "bxs", "bout", "bg1", "bg2", "bg3", "bg4", "bgt", "bgc",
                 "bpb", "bpev", "bw12", "brel12", "bw3", "brel3", "bh1", "bh2",
                 "bsil", "bhid", "bG", "bgev", "bsg", "bepi", "bpt2", "bps", "bpbc",
                 "bw12a", "bw12b", "bw12c", "bw12d", "bw12e"]:
        sems[name] = es.enter_context(nc.semaphore(name))

    def W(eng, name, v):
        if v > 0:
            eng.wait_ge(sems[name], v)

    def I(instr, name, k=1):
        return instr.then_inc(sems[name], k)

    AUX_N = 16 * (3 + 5 * NG)

    xnfv = xnf.ap().rearrange("p (g s t) -> p g s t", g=NG, s=2)

    def xnf_cur3(g):
        return xnfv[:, g, :, 1:TCA + 1]

    def xnf_sh3(g):
        return xnfv[:, g, :, 0:TCA]

    def flat3(tv, cols0=0):
        return tv.ap()[:, cols0:cols0 + CC].rearrange("p (s t) -> p s t", s=2)

    def c_kv(ci, m, which):
        return ci * 24 + m * 3 + which

    vraw_n = [0]

    with nc.Block() as blk:

        @blk.sync
        def _(sync):
            sync.dma_start(out=eye_sb[:], in_=eyeI).then_inc(sems["aux"], 16)
            sync.dma_start(out=oc_sb[:], in_=onesC).then_inc(sems["aux"], 16)
            sync.dma_start(out=or_sb[:], in_=onesR).then_inc(sems["aux"], 16)
            for g in range(NG):
                for t_sb, t_dr in [(w8t, tdec), (u8, tfir), (tk8, tmk), (tv8, tmv), (tr8, tmr)]:
                    sync.dma_start(out=t_sb[:, g:g + 1], in_=t_dr[g * P:(g + 1) * P, :],
                                   allow_slow_non_contiguous=True).then_inc(sems["aux"], 16)
                pass
            for ci in range(NCH):
                W(sync, "x_ld", 256 * ci)
                if ci >= 2:
                    W(sync, "ssp", 32 * 8 * (ci - 1))
                par = ci % 2
                for s in range(2):
                    for g in range(NG):
                        sync.dma_start(
                            out=xbuf[par][:, g * CC + s * TCA: g * CC + (s + 1) * TCA],
                            in_=xT[s, g * P:(g + 1) * P, ci * TCA:(ci + 1) * TCA],
                        ).then_inc(sems["x_ld"], 16)
            W(sync, "phA", PHA_TOTAL)
            W(sync, "phAD", 16)
            for j in range(NB):
                sj, t0 = j // NQ, (j % NQ) * TB
                for m in range(NG):
                    idx = j * 8 + m
                    if m >= 2:
                        mm = m - 2
                        W(sync, "bout", 16 * (j * 8 + mm))
                        W(sync, "bepi", j * 8 + mm + 1)
                        sync.dma_start(out=outT[sj, mm * P:(mm + 1) * P, t0:t0 + TB],
                                       in_=sxx[(j * 8 + mm) % 2][:]).then_inc(sems["bout"], 16)
                    W(sync, "bxs", 32 * idx)
                    if idx >= 2:
                        W(sync, "bepi", idx - 1)
                    sync.dma_start(out=sxs[idx % 2][:],
                                   in_=s_scr[sj, m * P:(m + 1) * P, t0:t0 + TB]
                                   ).then_inc(sems["bxs"], 16)
                    W(sync, "bout", 16 * (idx - 1))
                    sync.dma_start(out=sxx[idx % 2][:],
                                   in_=xT[sj, m * P:(m + 1) * P, t0:t0 + TB]
                                   ).then_inc(sems["bxs"], 16)
                for mm in (6, 7):
                    W(sync, "bout", 16 * (j * 8 + mm))
                    W(sync, "bepi", j * 8 + mm + 1)
                    sync.dma_start(out=outT[sj, mm * P:(mm + 1) * P, t0:t0 + TB],
                                   in_=sxx[(j * 8 + mm) % 2][:]).then_inc(sems["bout"], 16)
            sync.wait_ge(sems["bout"], 16 * 8 * NB)

        @blk.gpsimd
        def _(gp):
            for g in range(NG):
                gp.dma_start(out=gw_sb[:, g * E:(g + 1) * E],
                             in_=gwT[g * P:(g + 1) * P, :]).then_inc(sems["w_att"], 16)
            for w_sb_t, w_dr in [(wk_sb, wkT), (wv_sb, wvT), (wr_sb, wrT), (wo_sb, woT)]:
                for g in range(NG):
                    gp.dma_start(out=w_sb_t[:, g * D:(g + 1) * D],
                                 in_=w_dr[g * P:(g + 1) * P, :]).then_inc(sems["w_att"], 16)
            for ci in range(NCH):
                W(gp, "xnsp", 256 * ci)
                W(gp, "xnf_done", ci + 1)
                for s in range(2):
                    for g in range(NG):
                        gp.dma_start(out=xn_scr[s, g * P:(g + 1) * P, ci * TCA:(ci + 1) * TCA],
                                     in_=xnfv[:, g, s, 1:TCA + 1]).then_inc(sems["xnsp"], 16)
                for m in range(NG):
                    G = ci * 8 + m
                    p2 = G % 2
                    ab3 = abuf[p2].ap()[:, 0:2 * SS].rearrange("p (s t) -> p s t", s=2)
                    bb3 = bbuf[p2].ap()[:, 0:2 * SS].rearrange("p (s t) -> p s t", s=2)
                    W(gp, "vprod", G + 1)
                    gp.tensor_tensor(vg[p2][:], tmin[p2][:], vg[p2][:], ALU.mult)
                    gp.tensor_tensor(flat3(f1d[p2]), flat3(tmax[p2]), ab3[:, :, 0:TCA], ALU.mult)
                    gp.tensor_tensor(f1d[p2][:], f1d[p2][:], vg[p2][:], ALU.add)
                    gp.tensor_tensor(flat3(f2d[p2]), flat3(tmax[p2]), bb3[:, :, 0:TCA], ALU.mult)
                    gp.tensor_tensor(f2d[p2][:], f2d[p2][:], tmin[p2][:], ALU.add)
                    gp.tensor_tensor(tmin[p2][:], f2d[p2][:], erb[p2][:], ALU.mult)
                    I(gp.tensor_tensor(f2d[p2][:], f2d[p2][:], tmin[p2][:], ALU.add), "gpd")
            gp.wait_ge(sems["xnsp"], 16 * 16 * NCH)
            gp.dma_start(out=w8t[:, 0:1], in_=tdec[0:P, :],
                         allow_slow_non_contiguous=True).then_inc(sems["phAD"], 16)
            W(gp, "phA", PHA_TOTAL)
            W(gp, "phAD", 16)
            for j in range(NB):
                sj, t0 = j // NQ, (j % NQ) * TB
                if j >= 1:
                    W(gp, "bh2", 16 * E * NSUB * j)
                W(gp, "bxn", 128 * j)
                for g in range(NG):
                    gp.dma_start(out=xnb[:, g * TB:(g + 1) * TB],
                                 in_=xn_scr[sj, g * P:(g + 1) * P, t0:t0 + TB]
                                 ).then_inc(sems["bxn"], 16)
                for e in range(E):
                    L = j * E + e

                    def w12_load(hcp):
                        S = j * 128 + e * 16 + hcp
                        wsem = "bw12" + "abcde"[S % 5]
                        W(gp, wsem, 16 * (S // 5))
                        W(gp, "bh2", (S - 4) * NSUB)
                        gp.dma_start(out=w12_sb[S % 5][:, 0:2 * NG * P],
                                     in_=w12Hd[e, hcp]).then_inc(sems[wsem], 16)

                    for hcp in range(5):
                        w12_load(hcp)
                    W(gp, "bw3", 64 * L)
                    W(gp, "bG", L * 8 * NSUB)
                    for hq in range(4):
                        gp.dma_start(out=w3_sb[:, hq * 4 * D:(hq + 1) * 4 * D],
                                     in_=w3Hd[e, hq]).then_inc(sems["bw3"], 16)
                    for hcp in range(5, 16):
                        w12_load(hcp)
            if debug_taps:
                gp.wait_ge(sems["bepi"], 8 * NB)
                gp.dma_start(out=dbgB[:, 0:NG * TB], in_=gacc.ap()[:, 0:NG * TB]).then_inc(sems["aux"], 16)
                gp.dma_start(out=dbgB[:, NG * TB:NG * TB + TB], in_=PT2.ap()[:, 0:TB]).then_inc(sems["aux"], 16)
                gp.dma_start(out=dbgB[:, NG * TB + TB:NG * TB + 2 * TB], in_=P_sb.ap()[:, 0:TB]).then_inc(sems["aux"], 16)
                gp.dma_start(out=dbgB[0:E, NG * TB + 2 * TB:NG * TB + 3 * TB], in_=PT_sb[:]).then_inc(sems["aux"], 16)
                gp.dma_start(out=dbgB[:, NG * TB + 3 * TB:NG * TB + 3 * TB + NTT * E],
                             in_=Pn.ap()[:, 0:NTT * E]).then_inc(sems["aux"], 16)

        @blk.scalar
        def _(sc):
            W(sc, "aux", AUX_N)
            sc.activation(w8t[:], w8t[:], AF.Exp).then_inc(sems["sc_init"], 1)
            def sc_squares(ci):
                par = ci % 2
                W(sc, "mix", ci)
                W(sc, "xnsp", 16 * 16 * ci)
                W(sc, "x_ld", 16 * 16 * (ci + 1))
                for g in range(NG):
                    ins = sc.activation(xnf_cur3(g), flat3(xbuf[par], g * CC), AF.Square)
                I(ins, "sq")

            def sc_rstd(ci):
                W(sc, "row", ci + 1)
                sc.activation(rstdr, vrow, AF.Ln)
                I(sc.activation(rstdr, rstdr, AF.Exp, scale=-0.5), "rstd")

            sc_squares(0)
            sc_rstd(0)
            for ci in range(NCH):
                par = ci % 2
                for m in range(NG):
                    G = ci * 8 + m
                    pg = G % 2
                    W(sc, "grpdone", G - 1)
                    W(sc, "kv", c_kv(ci, m, 3))
                    I(sc.activation(erb[pg][:], psRr[pg][:], AF.Exp, scale=-1.0), "evr")
                    W(sc, "d2s", G + 1)
                    I(sc.activation(_pk[pg][:, 0:4 * CC], _pk[pg][:, 0:4 * CC], AF.Exp),
                      "s2d")
                    if ci + 1 < NCH:
                        if m == 2:
                            sc_squares(ci + 1)
                        if m == 3:
                            sc_rstd(ci + 1)
                W(sc, "xnf_done", ci + 1)
                W(sc, "stat", ci + 1)
                W(sc, "ssp", 256 * ci)
                for m in range(NG):
                    W(sc, "wo", ci * 8 + m + 1)
                    I(sc.copy(xbuf[par][:, m * CC:(m + 1) * CC], psS[m % 2][:]), "sev")
                    for s in range(2):
                        sc.dma_start(out=s_scr[s, m * P:(m + 1) * P, ci * TCA:(ci + 1) * TCA],
                                     in_=xbuf[par][:, m * CC + s * TCA:m * CC + (s + 1) * TCA]
                                     ).then_inc(sems["ssp"], 16)
            sc.wait_ge(sems["ssp"], 32 * 8 * NCH)
            I(sc.copy(rstdr[:, 0:4], vrow[:, 0:4]), "phA")
            W(sc, "phA", PHA_TOTAL)
            W(sc, "phAD", 16)
            def epi_silu(j, m):
                W(sc, "bgev", (j + 1) * E * 8 * NSUB)
                W(sc, "bepi", 8 * j + m)
                I(sc.activation(gs_sb[:], gacc[:, m * TB:(m + 1) * TB], AF.Silu), "bsg")

            nbat_ = (NTT + 3) // 4
            for j in range(NB):
                W(sc, "bg2", j + 1)
                for tt in range(NTT):
                    ins = sc.activation(ewb[:, tt * E:(tt + 1) * E],
                                        scn[:, tt * E:(tt + 1) * E], AF.Exp)
                I(ins, "bg3")
                W(sc, "bgc", nbat_ * (j + 1))
                poff = (j % 2) * TB
                sc.dma_start(out=p_scr[0:E, poff:poff + TB],
                             in_=PT_sb[:, 0:TB]).then_inc(sems["bps"], 16)
                spend = [m for m in range(NG)] if j > 0 else []
                for e in range(E):
                    W(sc, "bgev", (j * E + e) * 8 * NSUB)
                    W(sc, "bps", 16 * (j + 1))
                    sc.dma_start(out=P_sb.ap()[:, 0:TB],
                                 in_=p_scr[e:e + 1, poff:poff + TB].broadcast_to([P, TB])
                                 ).then_inc(sems["bpbc"], 16)
                    k = 0
                    for hcp in range(16):
                        for ts_ in range(NSUB):
                            U = (j * E + e) * 16 * NSUB + hcp * NSUB + ts_
                            W(sc, "bh1", U + 1)
                            W(sc, "bhid", U - 2)
                            I(sc.activation(s1t[U % 3][:], psH1[U % 3][:], AF.Silu), "bsil")
                            k += 1
                            if k % 4 == 0 and spend:
                                epi_silu(j - 1, spend.pop(0))
                    while spend:
                        epi_silu(j - 1, spend.pop(0))
            for m in range(NG):
                epi_silu(NB - 1, m)

        @blk.tensor
        def _(te):
            W(te, "aux", AUX_N)
            W(te, "w_att", 16 * (4 * NG + NG))
            def t_stats(ci):
                par = ci % 2
                W(te, "row", ci)
                W(te, "xnf_done", ci)
                W(te, "sev", 8 * ci - 8)
                W(te, "x_ld", 16 * 16 * (ci + 1))
                W(te, "sq", ci + 1)
                for g in range(NG):
                    te.matmul(psA[0:1, 0:CC], oc_sb[:], xbuf[par][:, g * CC:(g + 1) * CC],
                              start=(g == 0), stop=(g == NG - 1))
                for g in range(NG):
                    ins = te.matmul(psB[0:1, 0:CC], oc_sb[:], xnf_cur3(g),
                                    start=(g == 0), stop=(g == NG - 1))
                I(ins, "stat")

            def t_bcast(ci):
                W(te, "rstd", ci + 1)
                W(te, "row", ci + 1)
                te.matmul(psMb[:, 0:CC], or_sb[:], mrow, start=True, stop=True)
                I(te.matmul(psRb[:, 0:CC], or_sb[:], rstdr, start=True, stop=True), "bcast")

            t_stats(0)
            t_bcast(0)
            for ci in range(NCH):
                W(te, "mix", ci + 1)
                for m in range(NG):
                    G = ci * 8 + m
                    pg = G % 2
                    W(te, "d2s", G - 1)
                    for g in range(NG):
                        ins = te.matmul(psK[pg][:, 0:CC],
                                        wk_sb[:, g * D + m * P:g * D + (m + 1) * P],
                                        xk_sb[:, g * CC:(g + 1) * CC],
                                        start=(g == 0), stop=(g == NG - 1))
                    I(ins, "kv")
                    W(te, "kvcp", G - 1)
                    for g in range(NG):
                        ins = te.matmul(psV[pg][:, 0:CC],
                                        wv_sb[:, g * D + m * P:g * D + (m + 1) * P],
                                        xv_sb[:, g * CC:(g + 1) * CC],
                                        start=(g == 0), stop=(g == NG - 1))
                    I(ins, "kv")
                    W(te, "evr", G - 1)
                    for g in range(NG):
                        ins = te.matmul(psRr[pg][:, 0:CC],
                                        wr_sb[:, g * D + m * P:g * D + (m + 1) * P],
                                        xr_sb[:, g * CC:(g + 1) * CC],
                                        start=(g == 0), stop=(g == NG - 1))
                    I(ins, "kv")
                    if m == 3 and ci + 1 < NCH:
                        t_stats(ci + 1)
                        t_bcast(ci + 1)
                W(te, "grpdone", 8 * (ci + 1))
                for m in range(NG):
                    W(te, "sev", 8 * ci + m - 1)
                    for g in range(NG):
                        ins = te.matmul(psS[m % 2][:, 0:CC],
                                        wo_sb[:, g * D + m * P:g * D + (m + 1) * P],
                                        sry[:, g * CC:(g + 1) * CC],
                                        start=(g == 0), stop=(g == NG - 1))
                    I(ins, "wo")
            W(te, "sev", 8 * NCH)
            I(te.matmul(psA[0:1, 0:4], oc_sb[:], xbuf[0][:, 0:4], start=True, stop=True), "phA")
            W(te, "phA", PHA_TOTAL)
            W(te, "phAD", 16)
            nbat = (NTT + 3) // 4
            for j in range(NB):
                W(te, "bxn", 16 * 8 * (j + 1))
                W(te, "bgev", j * E * 8 * NSUB)
                if j > 0:
                    W(te, "bg4", j)
                for tt in range(NTT):
                    for g in range(NG):
                        ins = te.matmul(psSc[:, tt * E:(tt + 1) * E],
                                        xnb[:, g * TB + tt * P:g * TB + (tt + 1) * P],
                                        gw_sb[:, g * E:(g + 1) * E],
                                        start=(g == 0), stop=(g == NG - 1),
                                        skip_group_check=True)
                I(ins, "bg1")
                W(te, "bg4", j + 1)
                for bat in range(nbat):
                    W(te, "bgc", nbat * j + bat)
                    for q in range(min(4, NTT - bat * 4)):
                        tt = bat * 4 + q
                        ins = te.transpose(psPT[0:E, q * P:(q + 1) * P],
                                           Pn[:, tt * E:(tt + 1) * E], eye_sb[:])
                    I(ins, "bgt")
                for e in range(E):
                    if e == 0:
                        W(te, "bgc", nbat * (j + 1))
                    for hcp in range(16):
                        S = j * 128 + e * 16 + hcp
                        W(te, "bw12" + "abcde"[S % 5], 16 * (S // 5 + 1))
                        for ts_ in range(NSUB):
                            U = (j * E + e) * 16 * NSUB + hcp * NSUB + ts_
                            W(te, "bhid", U - 2)
                            for g in range(NG):
                                ins = te.matmul(psH1[U % 3][:],
                                                w12_sb[S % 5][:, g * P:(g + 1) * P],
                                                xnb[:, g * TB + ts_ * 512:g * TB + ts_ * 512 + 512],
                                                start=(g == 0), stop=(g == NG - 1))
                            I(ins, "bh1")
                            for g in range(NG):
                                ins = te.matmul(psH2[U % 3][:],
                                                w12_sb[S % 5][:, NG * P + g * P:NG * P + (g + 1) * P],
                                                xnb[:, g * TB + ts_ * 512:g * TB + ts_ * 512 + 512],
                                                start=(g == 0), stop=(g == NG - 1))
                            I(ins, "bh2")
                    W(te, "bw3", 64 * (j * E + e + 1))
                    W(te, "bhid", (j * E + e) * 16 * NSUB + 16 * NSUB)
                    if e == 0:
                        W(te, "bg2", j + 1)
                    for m in range(NG):
                        for ts_ in range(NSUB):
                            V = (j * E + e) * 8 * NSUB + m * NSUB + ts_
                            W(te, "bgev", V - 1)
                            for hc in range(NHC):
                                ins = te.matmul(psG[V % 2][:],
                                                w3_sb[:, hc * D + m * P:hc * D + (m + 1) * P],
                                                hid[:, hc * TB + ts_ * 512:hc * TB + ts_ * 512 + 512],
                                                start=(hc == 0), stop=(hc == NHC - 1))
                            I(ins, "bG")

        @blk.vector
        def _(v):
            def vwait():
                v.wait_ge(sems["vraw"], vraw_n[0])

            def vraw(ins):
                vraw_n[0] += 1
                return ins.then_inc(sems["vraw"], 1)

            v.memset(cp[:], -1e38)
            v.memset(ca[:], 0.0)
            v.memset(cb[:], 0.0)
            for g in range(NG):
                v.memset(xnfv[:, g, :, 0:1], 0.0)
            W(v, "sc_init", 1)
            v.tensor_scalar(w8[:], w8t[:], -1.0, None, ALU.mult)

            def round1(ci, m):
                G = ci * 8 + m
                pg = G % 2
                kps = psK[pg]
                k3 = kps.ap()[:, 0:CC].rearrange("p (s t) -> p s t", s=2)
                W(v, "kv", c_kv(ci, m, 2))
                I(v.tensor_copy(vg[pg][:], psV[pg][:, 0:CC]), "kvcp")
                v.tensor_copy(pbuf[pg][:, 0:1], cp[:, 2 * m:2 * m + 1])
                v.tensor_copy(pbuf[pg][:, SS:SS + 1], cp[:, 2 * m + 1:2 * m + 2])
                for s in range(2):
                    ins = v.tensor_tensor_scan(
                        pbuf[pg][:, s * SS + 1:(s + 1) * SS],
                        w8[:, m:m + 1].broadcast_to([P, TCA]),
                        kps[:, s * TCA:(s + 1) * TCA],
                        cp[:, 2 * m + s:2 * m + s + 1],
                        ALU.add, ALU.max)
                    vraw(ins)
                vwait()
                pv = pbuf[pg].ap()[:, 0:2 * SS].rearrange("p (s t) -> p s t", s=2)
                p3 = pv[:, :, 1:SS]
                psh3 = pv[:, :, 0:TCA]
                v.tensor_copy(cp[:, 2 * m:2 * m + 1], pbuf[pg][:, TCA:TCA + 1])
                v.tensor_copy(cp[:, 2 * m + 1:2 * m + 2], pbuf[pg][:, SS + TCA:SS + TCA + 1])
                v.scalar_tensor_tensor(flat3(f1d[pg]), psh3, w8[:, m:m + 1],
                                       p3, ALU.add, ALU.subtract)
                v.scalar_tensor_tensor(flat3(tB2), k3, u8[:, m:m + 1],
                                       psh3, ALU.add, ALU.max)
                v.tensor_tensor(flat3(f2d[pg]), k3, p3, ALU.subtract)
                v.tensor_tensor(flat3(tmax[pg]), psh3, flat3(tB2), ALU.subtract)
                I(v.scalar_tensor_tensor(flat3(tmin[pg]), k3, u8[:, m:m + 1],
                                         flat3(tB2), ALU.add, ALU.subtract), "d2s")

            def round2a(ci, m2):
                G2 = ci * 8 + m2
                p2 = G2 % 2
                ab = abuf[p2]
                bb = bbuf[p2]
                W(v, "s2d", G2 + 1)
                if m2 == 0:
                    W(v, "wo", 8 * ci)
                v.tensor_tensor(f2vb[:], f2d[p2][:], vg[p2][:], ALU.mult)
                v.tensor_copy(ab[:, 0:1], ca[:, 2 * m2:2 * m2 + 1])
                v.tensor_copy(ab[:, SS:SS + 1], ca[:, 2 * m2 + 1:2 * m2 + 2])
                v.tensor_copy(bb[:, 0:1], cb[:, 2 * m2:2 * m2 + 1])
                v.tensor_copy(bb[:, SS:SS + 1], cb[:, 2 * m2 + 1:2 * m2 + 2])
                for s in range(2):
                    v.tensor_tensor_scan(
                        ab.ap()[:, s * SS + 1:(s + 1) * SS],
                        f1d[p2][:, s * TCA:(s + 1) * TCA],
                        f2vb[:, s * TCA:(s + 1) * TCA],
                        ca[:, 2 * m2 + s:2 * m2 + s + 1],
                        ALU.mult, ALU.add)
                    ins = v.tensor_tensor_scan(
                        bb.ap()[:, s * SS + 1:(s + 1) * SS],
                        f1d[p2][:, s * TCA:(s + 1) * TCA],
                        f2d[p2][:, s * TCA:(s + 1) * TCA],
                        cb[:, 2 * m2 + s:2 * m2 + s + 1],
                        ALU.mult, ALU.add)
                    vraw(ins)
                vwait()
                v.tensor_copy(ca[:, 2 * m2:2 * m2 + 1], ab[:, TCA:TCA + 1])
                v.tensor_copy(ca[:, 2 * m2 + 1:2 * m2 + 2], ab[:, SS + TCA:SS + TCA + 1])
                v.tensor_copy(cb[:, 2 * m2:2 * m2 + 1], bb[:, TCA:TCA + 1])
                I(v.tensor_copy(cb[:, 2 * m2 + 1:2 * m2 + 2], bb[:, SS + TCA:SS + TCA + 1]),
                  "vprod")

            def round2b(ci, m2):
                G2 = ci * 8 + m2
                p2 = G2 % 2
                W(v, "gpd", G2 + 1)
                v.reciprocal(tA2[:], f2d[p2][:])
                I(v.tensor_tensor(sry[:, m2 * CC:(m2 + 1) * CC], f1d[p2][:], tA2[:], ALU.mult),
                  "grpdone")

            def v_rows(ci):
                W(v, "stat", ci + 1)
                v.tensor_scalar(mrow, psA[0:1, 0:CC], 1.0 / D, None, ALU.mult)
                v.tensor_scalar(vrow, psB[0:1, 0:CC], 1.0 / D, None, ALU.mult)
                v.tensor_tensor(rstdr, mrow, mrow, ALU.mult)
                v.tensor_tensor(vrow, vrow, rstdr, ALU.subtract)
                I(v.tensor_scalar(vrow, vrow, 1e-5, None, ALU.add), "row")

            def v_xn(ci):
                par = ci % 2
                W(v, "bcast", ci + 1)
                W(v, "xnsp", 256 * ci)
                psM3 = psMb.ap()[:, 0:CC].rearrange("p (s t) -> p s t", s=2)
                psR3 = psRb.ap()[:, 0:CC].rearrange("p (s t) -> p s t", s=2)
                for g in range(NG):
                    v.tensor_tensor(flat3(t0buf), flat3(xbuf[par], g * CC), psM3, ALU.subtract)
                    ins = v.tensor_tensor(xnf_cur3(g), flat3(t0buf), psR3, ALU.mult)
                I(ins, "xnf_done")

            def v_mix(ci):
                W(v, "kv", 24 * ci)
                for g in range(NG):
                    v.tensor_tensor(flat3(dbuf), xnf_cur3(g), xnf_sh3(g), ALU.subtract)
                    v.scalar_tensor_tensor(flat3(xk_sb, g * CC), flat3(dbuf), tk8[:, g:g + 1],
                                           xnf_sh3(g), ALU.mult, ALU.add)
                    v.scalar_tensor_tensor(flat3(xv_sb, g * CC), flat3(dbuf), tv8[:, g:g + 1],
                                           xnf_sh3(g), ALU.mult, ALU.add)
                    v.scalar_tensor_tensor(flat3(xr_sb, g * CC), flat3(dbuf), tr8[:, g:g + 1],
                                           xnf_sh3(g), ALU.mult, ALU.add)
                for g in range(NG):
                    ins = v.tensor_copy(xnfv[:, g, :, 0:1], xnfv[:, g, :, TCA:TCA + 1])
                I(ins, "mix")

            v_rows(0)
            v_xn(0)
            v_mix(0)
            for ci in range(NCH):
                for m in range(NG):
                    if m >= 2:
                        round2b(ci, m - 2)
                    round1(ci, m)
                    if m >= 1:
                        round2a(ci, m - 1)
                    if ci + 1 < NCH:
                        if m == 3:
                            v_rows(ci + 1)
                        elif m == 4:
                            v_xn(ci + 1)
                        elif m == 5:
                            v_mix(ci + 1)
                round2b(ci, 6)
                round2a(ci, 7)
                round2b(ci, 7)
            I(v.memset(dbuf[:, 0:4], 0.0), "phA")
            W(v, "phA", PHA_TOTAL)
            W(v, "phAD", 16)
            nbat = (NTT + 3) // 4

            def epi_piece(j, m):
                idx = j * 8 + m
                W(v, "bxs", 32 * (idx + 1))
                W(v, "bsg", 8 * j + m + 1)
                v.tensor_tensor(obuf[:], gs_sb[:], sxs[idx % 2][:], ALU.mult)
                I(v.tensor_tensor(sxx[idx % 2][:], sxx[idx % 2][:], obuf[:], ALU.add),
                  "bepi")

            for j in range(NB):
                W(v, "bg1", j + 1)
                for tt in range(NTT):
                    v.reduce_max(m1n8[:, tt:tt + 1], psSc[:, tt * E:(tt + 1) * E],
                                 axis=AX.X, negate=True)
                for tt in range(NTT):
                    ins = v.tensor_scalar(scn[:, tt * E:(tt + 1) * E],
                                          psSc[:, tt * E:(tt + 1) * E],
                                          m1n8[:, tt:tt + 1], None, ALU.add)
                I(ins, "bg2")
                for tt in range(NTT):
                    v.tensor_scalar(sc2[:, tt * E:(tt + 1) * E], scn[:, tt * E:(tt + 1) * E],
                                    0.0, -1e30, ALU.is_ge, ALU.mult)
                for tt in range(NTT):
                    v.tensor_tensor(sc2[:, tt * E:(tt + 1) * E], scn[:, tt * E:(tt + 1) * E],
                                    sc2[:, tt * E:(tt + 1) * E], ALU.add)
                for tt in range(NTT):
                    v.reduce_max(m28[:, tt:tt + 1], sc2[:, tt * E:(tt + 1) * E], axis=AX.X)
                for tt in range(NTT):
                    v.tensor_scalar(mkb[:, tt * E:(tt + 1) * E], scn[:, tt * E:(tt + 1) * E],
                                    m28[:, tt:tt + 1], None, ALU.is_ge)
                W(v, "bg3", j + 1)
                for tt in range(NTT):
                    v.tensor_tensor(w8b[:, tt * E:(tt + 1) * E], ewb[:, tt * E:(tt + 1) * E],
                                    mkb[:, tt * E:(tt + 1) * E], ALU.mult)
                for tt in range(NTT):
                    ins = v.reduce_sum(z8[:, tt:tt + 1], w8b[:, tt * E:(tt + 1) * E], axis=AX.X)
                vraw(ins)
                vwait()
                vraw(v.reciprocal(rz8[:], z8[:]))
                vwait()
                for tt in range(NTT):
                    ins = v.tensor_scalar(Pn[:, tt * E:(tt + 1) * E], w8b[:, tt * E:(tt + 1) * E],
                                          rz8[:, tt:tt + 1], None, ALU.mult)
                I(ins, "bg4")
                W(v, "bps", 16 * j)
                for bat in range(nbat):
                    W(v, "bgt", nbat * j + bat + 1)
                    nq = min(4, NTT - bat * 4)
                    I(v.tensor_copy(PT_sb[:, bat * 512:bat * 512 + nq * P],
                                    psPT[0:E, 0:nq * P]), "bgc")
                pend = [m for m in range(NG)] if j > 0 else []
                for e in range(E):
                    k = 0
                    for hcp in range(16):
                        for ts_ in range(NSUB):
                            U = (j * E + e) * 16 * NSUB + hcp * NSUB + ts_
                            W(v, "bsil", U + 1)
                            W(v, "bh2", U + 1)
                            I(v.scalar_tensor_tensor(
                                hid[:, hcp * TB + ts_ * 512:hcp * TB + ts_ * 512 + 512],
                                s1t[U % 3][:], 1.0, psH2[U % 3][:], ALU.bypass, ALU.mult),
                              "bhid")
                            k += 1
                            if k % 4 == 0 and pend:
                                epi_piece(j - 1, pend.pop(0))
                    while pend:
                        epi_piece(j - 1, pend.pop(0))
                    W(v, "bpbc", 16 * (j * E + e + 1))
                    if e == 0:
                        W(v, "bsg", 8 * j)
                    for m in range(NG):
                        for ts_ in range(NSUB):
                            V = (j * E + e) * 8 * NSUB + m * NSUB + ts_
                            W(v, "bG", V + 1)
                            gsl = gacc[:, m * TB + ts_ * 512:m * TB + ts_ * 512 + 512]
                            if e == 0:
                                I(v.tensor_tensor(gsl, psG[V % 2][:],
                                                  P_sb[:, ts_ * 512:(ts_ + 1) * 512],
                                                  ALU.mult), "bgev")
                            else:
                                v.tensor_tensor(bt1[:], psG[V % 2][:],
                                                P_sb[:, ts_ * 512:(ts_ + 1) * 512], ALU.mult)
                                I(v.tensor_tensor(gsl, gsl, bt1[:], ALU.add), "bgev")
            for m in range(NG):
                epi_piece(NB - 1, m)

    nc.finalize()
    return nc


_NC_CACHE = {}


def _get_nc(T):
    if T not in _NC_CACHE:
        _NC_CACHE[T] = build(T=T)
    return _NC_CACHE[T]


def _prep(L0_attn_x, L0_attn_time_decay, L0_attn_time_first, L0_attn_time_mix_k,
          L0_attn_time_mix_v, L0_attn_time_mix_r, L0_attn_Wk, L0_attn_Wv,
          L0_attn_Wr, L0_attn_Wo, L0_ffn_w12, L0_ffn_w3, L0_ffn_gate_w):
    x = np.asarray(L0_attn_x, dtype=np.float32)
    B, T, C = x.shape
    xT = np.ascontiguousarray(x.transpose(0, 2, 1))
    w12 = np.asarray(L0_ffn_w12, np.float32)
    w3 = np.asarray(L0_ffn_w3, np.float32)
    shared = dict(
        wkT=np.ascontiguousarray(np.asarray(L0_attn_Wk, np.float32).T),
        wvT=np.ascontiguousarray(np.asarray(L0_attn_Wv, np.float32).T),
        wrT=np.ascontiguousarray(np.asarray(L0_attn_Wr, np.float32).T),
        woT=np.ascontiguousarray(np.asarray(L0_attn_Wo, np.float32).T),
        w12H=(lambda w12H_: np.ascontiguousarray(
            np.concatenate([w12H_[:, :H // P], w12H_[:, H // P:]], axis=3)))(
            w12.reshape(E, 2 * H // P, P, NG, P).transpose(0, 1, 4, 3, 2)
            .reshape(E, 2 * H // P, P, NG * P)),
        w3H=np.ascontiguousarray(
            w3.reshape(E, NG, P, H // P, P).transpose(0, 3, 4, 1, 2)
            .reshape(E, 4, H // P // 4, P, NG * P).transpose(0, 1, 3, 2, 4)
            .reshape(E, 4, P, (H // P // 4) * NG * P)),
        gwT=np.ascontiguousarray(np.asarray(L0_ffn_gate_w, np.float32).T),
        tdec=np.ascontiguousarray(np.asarray(L0_attn_time_decay, np.float32).reshape(D, 1)),
        tfir=np.ascontiguousarray(np.asarray(L0_attn_time_first, np.float32).reshape(D, 1)),
        tmk=np.ascontiguousarray(np.asarray(L0_attn_time_mix_k, np.float32).reshape(D, 1)),
        tmv=np.ascontiguousarray(np.asarray(L0_attn_time_mix_v, np.float32).reshape(D, 1)),
        tmr=np.ascontiguousarray(np.asarray(L0_attn_time_mix_r, np.float32).reshape(D, 1)),
        eyeI=np.eye(P, dtype=np.float32),
        onesC=np.ones((P, 1), np.float32),
        onesR=np.ones((1, P), np.float32),
    )
    return xT, shared, B, T


def kernel(**inputs):
    xT, shared, B, T = _prep(**inputs)
    n_cores = 8
    bpc = B // n_cores
    nc = _get_nc(T)
    in_maps = [dict(xT=np.ascontiguousarray(xT[c * bpc:(c + 1) * bpc]), **shared)
               for c in range(n_cores)]
    res = run_bass_kernel_spmd(nc, in_maps, core_ids=list(range(n_cores)))
    outT = np.concatenate([res.results[c]["outT"] for c in range(n_cores)], axis=0)
    return np.ascontiguousarray(outT.transpose(0, 2, 1))


# version 12
# speedup vs baseline: 1.0707x; 1.0707x over previous
import numpy as np
from contextlib import ExitStack

import concourse.bass as bass
import concourse.mybir as mybir
from concourse.bass_utils import run_bass_kernel_spmd

F32 = mybir.dt.float32
BF16 = mybir.dt.bfloat16
AF = mybir.ActivationFunctionType
ALU = mybir.AluOpType
AX = mybir.AxisListType

D = 1024
H = 2048
E = 8
NG = 8
P = 128


class TView:

    def __init__(self, ap):
        self._ap = ap

    def __getitem__(self, idx):
        return self._ap[idx]

    def ap(self):
        return self._ap


class Bump:
    def __init__(self, blob, total_cols):
        self.blob = blob
        self.off = 0
        self.total = total_cols
        self.hwm = 0

    def take(self, cols):
        cols = (cols + 15) // 16 * 16
        ap = self.blob.ap()[:, self.off:self.off + cols]
        self.off += cols
        self.hwm = max(self.hwm, self.off)
        assert self.off <= self.total, f"arena overflow {self.off} > {self.total}"
        return TView(ap)

    def reset(self):
        self.off = 0


def build(T=4096, TCA=256, TBQ=1024, debug_taps=False, detect_races=True):
    TB = min(TBQ, T)
    NCH = T // TCA
    CC = 2 * TCA
    SS = TCA + 1
    NQ = T // TB
    NB = 2 * NQ
    NSUB = TB // 512
    NTT = TB // 128
    NHC = H // P
    PHA_TOTAL = 3

    nc = bass.Bass("TRN2", target_bir_lowering=False,
                   dynamic_dma_scratch_size=16384,
                   detect_race_conditions=detect_races)

    xT = nc.dram_tensor("xT", [2, D, T], F32, kind="ExternalInput").ap()
    wkT = nc.dram_tensor("wkT", [D, D], F32, kind="ExternalInput").ap()
    wvT = nc.dram_tensor("wvT", [D, D], F32, kind="ExternalInput").ap()
    wrT = nc.dram_tensor("wrT", [D, D], F32, kind="ExternalInput").ap()
    woT = nc.dram_tensor("woT", [D, D], F32, kind="ExternalInput").ap()
    w12Hd = nc.dram_tensor("w12H", [E, H // P, P, 2 * NG * P], F32, kind="ExternalInput").ap()
    w3Hd = nc.dram_tensor("w3H", [E, NHC // 4, P, 4 * NG * P], F32, kind="ExternalInput").ap()
    gwT = nc.dram_tensor("gwT", [D, E], F32, kind="ExternalInput").ap()
    tdec = nc.dram_tensor("tdec", [D, 1], F32, kind="ExternalInput").ap()
    tfir = nc.dram_tensor("tfir", [D, 1], F32, kind="ExternalInput").ap()
    tmk = nc.dram_tensor("tmk", [D, 1], F32, kind="ExternalInput").ap()
    tmv = nc.dram_tensor("tmv", [D, 1], F32, kind="ExternalInput").ap()
    tmr = nc.dram_tensor("tmr", [D, 1], F32, kind="ExternalInput").ap()
    eyeI = nc.dram_tensor("eyeI", [P, P], F32, kind="ExternalInput").ap()
    onesC = nc.dram_tensor("onesC", [P, 1], F32, kind="ExternalInput").ap()
    onesR = nc.dram_tensor("onesR", [1, P], F32, kind="ExternalInput").ap()

    outT = nc.dram_tensor("outT", [2, D, T], F32, kind="ExternalOutput").ap()
    _scrk = "ExternalOutput" if debug_taps else "Internal"
    xn_scr = nc.dram_tensor("xn_scr", [2, D, T], BF16, kind=_scrk).ap()
    p_scr = nc.dram_tensor("p_scr", [E, 2 * TB], F32, kind="Internal").ap()
    s_scr = nc.dram_tensor("s_scr", [2, D, T], F32, kind=_scrk).ap()
    if debug_taps:
        dbgB = nc.dram_tensor("dbgB", [P, NG * TBQ + 3 * TBQ + 64], F32, kind="ExternalOutput").ap()

    es = ExitStack()

    def sb(name, shape, dt):
        return es.enter_context(nc.sbuf_tensor(name, shape, dt))

    def psf(name):
        return es.enter_context(nc.psum_tensor(name, [P, 512], F32))

    w8t = sb("w8t", [P, NG], F32)
    w8 = sb("w8", [P, NG], F32)
    u8 = sb("u8", [P, NG], F32)
    tk8 = sb("tk8", [P, NG], F32)
    tv8 = sb("tv8", [P, NG], F32)
    tr8 = sb("tr8", [P, NG], F32)
    cp = sb("cp", [P, 2 * NG], F32)
    cab = sb("cab", [P, 4 * NG], F32)
    or_sb = sb("or_sb", [1, P], F32)
    oc_sb = sb("oc_sb", [P, 1], F32)
    eye_sb = sb("eye_sb", [P, P], F32)
    rows = sb("rows", [1, 3 * CC], F32)
    gw_sb = sb("gw_sb", [P, E * NG], BF16)

    mrow = rows.ap()[:, 0:CC]
    vrow = rows.ap()[:, CC:2 * CC]
    rstdr = rows.ap()[:, 2 * CC:3 * CC]

    blob32 = sb("blob32", [P, 24688], F32)
    blob16 = sb("blob16", [P, 51200], BF16)
    a32 = Bump(blob32, 24688)
    a16 = Bump(blob16, 51200)

    xbuf = [a32.take(NG * CC), a32.take(NG * CC)]
    xnf = a32.take(NG * 2 * SS)
    kg = [a32.take(CC), a32.take(CC)]
    vg = [a32.take(CC), a32.take(CC)]
    pbuf = [a32.take(2 * SS), a32.take(2 * SS)]
    qbuf = [a32.take(4 * SS), a32.take(4 * SS)]
    _pk = [a32.take(4 * CC), a32.take(4 * CC)]
    f1d = [TView(_pk[p].ap()[:, 0:CC]) for p in range(2)]
    f2d = [TView(_pk[p].ap()[:, CC:2 * CC]) for p in range(2)]
    tmin = [TView(_pk[p].ap()[:, 2 * CC:3 * CC]) for p in range(2)]
    tmax = [TView(_pk[p].ap()[:, 3 * CC:4 * CC]) for p in range(2)]
    erb = [a32.take(CC), a32.take(CC)]
    f2vb = a32.take(CC)
    tA2 = a32.take(CC)
    tB2 = a32.take(CC)
    dbuf = a32.take(CC)
    t0buf = dbuf
    A32_HWM = a32.hwm

    wk_sb = a16.take(NG * D)
    wv_sb = a16.take(NG * D)
    wr_sb = a16.take(NG * D)
    wo_sb = a16.take(NG * D)
    xk_sb = a16.take(NG * CC)
    xv_sb = a16.take(NG * CC)
    xr_sb = a16.take(NG * CC)
    sry = a16.take(NG * CC)
    A16_HWM = a16.hwm

    a32.reset()
    a16.reset()
    gacc = a32.take(NG * TB)
    sxs = [a32.take(TB), a32.take(TB)]
    sxx = [a32.take(TB), a32.take(TB)]
    obuf = a32.take(TB)
    bt1 = a32.take(512)
    P_sb = a32.take(TB)
    gs_sb = a32.take(TB)
    s1t = [a32.take(512), a32.take(512), a32.take(512)]
    scn = a32.take(NTT * E)
    sc2 = a32.take(NTT * E)
    mkb = a32.take(NTT * E)
    w8b = a32.take(NTT * E)
    ewb = a32.take(NTT * E)
    Pn = a32.take(NTT * E)
    m1n8 = a32.take(NTT)
    m28 = a32.take(NTT)
    z8 = a32.take(NTT)
    rz8 = a32.take(NTT)
    PT2 = a32.take(TB)
    PT_sb = TView(a32.take(TB).ap()[0:E, :])
    B32_HWM = a32.hwm

    xnb = a16.take(NG * TB)
    hid = a16.take(NHC * TB)
    w12_sb = [a16.take(2 * NG * P) for _ in range(5)]
    w3_sb = a16.take(NHC * D)
    B16_HWM = a16.hwm

    psA = psf("psA")
    psB = psf("psB")
    psM = psf("psM")
    psR_ = psf("psR_")
    psV0 = psf("psV0")
    psV1 = psf("psV1")
    psR0 = psf("psR0")
    psR1 = psf("psR1")
    psK = [psM, psR_]
    psV = [psV0, psV1]
    psS = [psA, psB]
    psMb = psA
    psRb = psB
    psRr = [psR0, psR1]
    psG = [psA, psB]
    psH1 = [psM, psR_, psV0]
    psH2 = [psV1, psR0, psR1]
    psSc = psA
    psPT = psB

    sems = {}
    for name in ["aux", "w_att", "sc_init", "x_ld", "sq", "stat", "row",
                 "rstd", "bcast", "xnf_done", "mix", "kv", "evr", "d2s", "s2d",
                 "grpdone", "wo", "sev", "ssp", "xnsp", "phA", "phAD", "vraw", "kvcp",
                 "vprod", "gpd",
                 "bxn", "bxs", "bout", "bg1", "bg2", "bg3", "bg4", "bgt", "bgc",
                 "bpb", "bpev", "bw12", "brel12", "bw3", "brel3", "bh1", "bh2",
                 "bsil", "bhid", "bG", "bgev", "bsg", "bepi", "bpt2", "bps", "bpbc",
                 "bw12a", "bw12b", "bw12c", "bw12d", "bw12e"]:
        sems[name] = es.enter_context(nc.semaphore(name))

    def W(eng, name, v):
        if v > 0:
            eng.wait_ge(sems[name], v)

    def I(instr, name, k=1):
        return instr.then_inc(sems[name], k)

    AUX_N = 16 * (3 + 5 * NG)

    xnfv = xnf.ap().rearrange("p (g s t) -> p g s t", g=NG, s=2)

    def xnf_cur3(g):
        return xnfv[:, g, :, 1:TCA + 1]

    def xnf_sh3(g):
        return xnfv[:, g, :, 0:TCA]

    def flat3(tv, cols0=0):
        return tv.ap()[:, cols0:cols0 + CC].rearrange("p (s t) -> p s t", s=2)

    def c_kv(ci, m, which):
        return ci * 24 + m * 3 + which

    vraw_n = [0]

    with nc.Block() as blk:

        @blk.sync
        def _(sync):
            sync.dma_start(out=eye_sb[:], in_=eyeI).then_inc(sems["aux"], 16)
            sync.dma_start(out=oc_sb[:], in_=onesC).then_inc(sems["aux"], 16)
            sync.dma_start(out=or_sb[:], in_=onesR).then_inc(sems["aux"], 16)
            for g in range(NG):
                for t_sb, t_dr in [(w8t, tdec), (u8, tfir), (tk8, tmk), (tv8, tmv), (tr8, tmr)]:
                    sync.dma_start(out=t_sb[:, g:g + 1], in_=t_dr[g * P:(g + 1) * P, :],
                                   allow_slow_non_contiguous=True).then_inc(sems["aux"], 16)
                pass
            for ci in range(NCH):
                W(sync, "x_ld", 256 * ci)
                if ci >= 2:
                    W(sync, "ssp", 32 * 8 * (ci - 1))
                par = ci % 2
                for s in range(2):
                    for g in range(NG):
                        sync.dma_start(
                            out=xbuf[par][:, g * CC + s * TCA: g * CC + (s + 1) * TCA],
                            in_=xT[s, g * P:(g + 1) * P, ci * TCA:(ci + 1) * TCA],
                        ).then_inc(sems["x_ld"], 16)
            W(sync, "phA", PHA_TOTAL)
            W(sync, "phAD", 16)
            for j in range(NB):
                sj, t0 = j // NQ, (j % NQ) * TB
                for m in range(NG):
                    idx = j * 8 + m
                    if m >= 2:
                        mm = m - 2
                        W(sync, "bout", 16 * (j * 8 + mm))
                        W(sync, "bepi", j * 8 + mm + 1)
                        sync.dma_start(out=outT[sj, mm * P:(mm + 1) * P, t0:t0 + TB],
                                       in_=sxx[(j * 8 + mm) % 2][:]).then_inc(sems["bout"], 16)
                    W(sync, "bxs", 32 * idx)
                    if idx >= 2:
                        W(sync, "bepi", idx - 1)
                    sync.dma_start(out=sxs[idx % 2][:],
                                   in_=s_scr[sj, m * P:(m + 1) * P, t0:t0 + TB]
                                   ).then_inc(sems["bxs"], 16)
                    W(sync, "bout", 16 * (idx - 1))
                    sync.dma_start(out=sxx[idx % 2][:],
                                   in_=xT[sj, m * P:(m + 1) * P, t0:t0 + TB]
                                   ).then_inc(sems["bxs"], 16)
                for mm in (6, 7):
                    W(sync, "bout", 16 * (j * 8 + mm))
                    W(sync, "bepi", j * 8 + mm + 1)
                    sync.dma_start(out=outT[sj, mm * P:(mm + 1) * P, t0:t0 + TB],
                                   in_=sxx[(j * 8 + mm) % 2][:]).then_inc(sems["bout"], 16)
            sync.wait_ge(sems["bout"], 16 * 8 * NB)

        @blk.gpsimd
        def _(gp):
            for g in range(NG):
                gp.dma_start(out=gw_sb[:, g * E:(g + 1) * E],
                             in_=gwT[g * P:(g + 1) * P, :]).then_inc(sems["w_att"], 16)
            for w_sb_t, w_dr in [(wk_sb, wkT), (wv_sb, wvT), (wr_sb, wrT), (wo_sb, woT)]:
                for g in range(NG):
                    gp.dma_start(out=w_sb_t[:, g * D:(g + 1) * D],
                                 in_=w_dr[g * P:(g + 1) * P, :]).then_inc(sems["w_att"], 16)
            for ci in range(NCH):
                W(gp, "xnsp", 256 * ci)
                W(gp, "xnf_done", ci + 1)
                for s in range(2):
                    for g in range(NG):
                        gp.dma_start(out=xn_scr[s, g * P:(g + 1) * P, ci * TCA:(ci + 1) * TCA],
                                     in_=xnfv[:, g, s, 1:TCA + 1]).then_inc(sems["xnsp"], 16)
                for m in range(NG):
                    G = ci * 8 + m
                    p2 = G % 2
                    ab3 = qbuf[p2].ap()[:, 0:2 * SS].rearrange("p (s t) -> p s t", s=2)
                    bb3 = qbuf[p2].ap()[:, 2 * SS:4 * SS].rearrange("p (s t) -> p s t", s=2)
                    W(gp, "vprod", G + 1)
                    gp.tensor_tensor(vg[p2][:], tmin[p2][:], vg[p2][:], ALU.mult)
                    gp.tensor_tensor(flat3(f1d[p2]), flat3(tmax[p2]), ab3[:, :, 0:TCA], ALU.mult)
                    gp.tensor_tensor(f1d[p2][:], f1d[p2][:], vg[p2][:], ALU.add)
                    gp.tensor_tensor(flat3(f2d[p2]), flat3(tmax[p2]), bb3[:, :, 0:TCA], ALU.mult)
                    gp.tensor_tensor(f2d[p2][:], f2d[p2][:], tmin[p2][:], ALU.add)
                    gp.tensor_tensor(tmin[p2][:], f2d[p2][:], erb[p2][:], ALU.mult)
                    I(gp.tensor_tensor(f2d[p2][:], f2d[p2][:], tmin[p2][:], ALU.add), "gpd")
            gp.wait_ge(sems["xnsp"], 16 * 16 * NCH)
            gp.dma_start(out=w8t[:, 0:1], in_=tdec[0:P, :],
                         allow_slow_non_contiguous=True).then_inc(sems["phAD"], 16)
            W(gp, "phA", PHA_TOTAL)
            W(gp, "phAD", 16)
            for j in range(NB):
                sj, t0 = j // NQ, (j % NQ) * TB
                if j >= 1:
                    W(gp, "bh2", 16 * E * NSUB * j)
                W(gp, "bxn", 128 * j)
                for g in range(NG):
                    gp.dma_start(out=xnb[:, g * TB:(g + 1) * TB],
                                 in_=xn_scr[sj, g * P:(g + 1) * P, t0:t0 + TB]
                                 ).then_inc(sems["bxn"], 16)
                for e in range(E):
                    L = j * E + e

                    def w12_load(hcp):
                        S = j * 128 + e * 16 + hcp
                        wsem = "bw12" + "abcde"[S % 5]
                        W(gp, wsem, 16 * (S // 5))
                        W(gp, "bh2", (S - 4) * NSUB)
                        gp.dma_start(out=w12_sb[S % 5][:, 0:2 * NG * P],
                                     in_=w12Hd[e, hcp]).then_inc(sems[wsem], 16)

                    for hcp in range(5):
                        w12_load(hcp)
                    W(gp, "bw3", 64 * L)
                    W(gp, "bG", L * 8 * NSUB)
                    for hq in range(4):
                        gp.dma_start(out=w3_sb[:, hq * 4 * D:(hq + 1) * 4 * D],
                                     in_=w3Hd[e, hq]).then_inc(sems["bw3"], 16)
                    for hcp in range(5, 16):
                        w12_load(hcp)
            if debug_taps:
                gp.wait_ge(sems["bepi"], 8 * NB)
                gp.dma_start(out=dbgB[:, 0:NG * TB], in_=gacc.ap()[:, 0:NG * TB]).then_inc(sems["aux"], 16)
                gp.dma_start(out=dbgB[:, NG * TB:NG * TB + TB], in_=PT2.ap()[:, 0:TB]).then_inc(sems["aux"], 16)
                gp.dma_start(out=dbgB[:, NG * TB + TB:NG * TB + 2 * TB], in_=P_sb.ap()[:, 0:TB]).then_inc(sems["aux"], 16)
                gp.dma_start(out=dbgB[0:E, NG * TB + 2 * TB:NG * TB + 3 * TB], in_=PT_sb[:]).then_inc(sems["aux"], 16)
                gp.dma_start(out=dbgB[:, NG * TB + 3 * TB:NG * TB + 3 * TB + NTT * E],
                             in_=Pn.ap()[:, 0:NTT * E]).then_inc(sems["aux"], 16)

        @blk.scalar
        def _(sc):
            W(sc, "aux", AUX_N)
            sc.activation(w8t[:], w8t[:], AF.Exp).then_inc(sems["sc_init"], 1)
            def sc_squares(ci):
                par = ci % 2
                W(sc, "mix", ci)
                W(sc, "xnsp", 16 * 16 * ci)
                W(sc, "x_ld", 16 * 16 * (ci + 1))
                for g in range(NG):
                    ins = sc.activation(xnf_cur3(g), flat3(xbuf[par], g * CC), AF.Square)
                I(ins, "sq")

            def sc_rstd(ci):
                W(sc, "row", ci + 1)
                sc.activation(rstdr, vrow, AF.Ln)
                I(sc.activation(rstdr, rstdr, AF.Exp, scale=-0.5), "rstd")

            sc_squares(0)
            sc_rstd(0)
            for ci in range(NCH):
                par = ci % 2
                for m in range(NG):
                    G = ci * 8 + m
                    pg = G % 2
                    W(sc, "grpdone", G - 1)
                    W(sc, "kv", c_kv(ci, m, 3))
                    I(sc.activation(erb[pg][:], psRr[pg][:], AF.Exp, scale=-1.0), "evr")
                    W(sc, "d2s", G + 1)
                    I(sc.activation(_pk[pg][:, 0:4 * CC], _pk[pg][:, 0:4 * CC], AF.Exp),
                      "s2d")
                    if ci + 1 < NCH:
                        if m == 2:
                            sc_squares(ci + 1)
                        if m == 3:
                            sc_rstd(ci + 1)
                W(sc, "xnf_done", ci + 1)
                W(sc, "stat", ci + 1)
                W(sc, "ssp", 256 * ci)
                for m in range(NG):
                    W(sc, "wo", ci * 8 + m + 1)
                    I(sc.copy(xbuf[par][:, m * CC:(m + 1) * CC], psS[m % 2][:]), "sev")
                    for s in range(2):
                        sc.dma_start(out=s_scr[s, m * P:(m + 1) * P, ci * TCA:(ci + 1) * TCA],
                                     in_=xbuf[par][:, m * CC + s * TCA:m * CC + (s + 1) * TCA]
                                     ).then_inc(sems["ssp"], 16)
            sc.wait_ge(sems["ssp"], 32 * 8 * NCH)
            I(sc.copy(rstdr[:, 0:4], vrow[:, 0:4]), "phA")
            W(sc, "phA", PHA_TOTAL)
            W(sc, "phAD", 16)
            def epi_silu(j, m):
                W(sc, "bgev", (j + 1) * E * 8 * NSUB)
                W(sc, "bepi", 8 * j + m)
                I(sc.activation(gs_sb[:], gacc[:, m * TB:(m + 1) * TB], AF.Silu), "bsg")

            nbat_ = (NTT + 3) // 4
            for j in range(NB):
                W(sc, "bg2", j + 1)
                for tt in range(NTT):
                    ins = sc.activation(ewb[:, tt * E:(tt + 1) * E],
                                        scn[:, tt * E:(tt + 1) * E], AF.Exp)
                I(ins, "bg3")
                W(sc, "bgc", nbat_ * (j + 1))
                poff = (j % 2) * TB
                sc.dma_start(out=p_scr[0:E, poff:poff + TB],
                             in_=PT_sb[:, 0:TB]).then_inc(sems["bps"], 16)
                spend = [m for m in range(NG)] if j > 0 else []
                for e in range(E):
                    W(sc, "bgev", (j * E + e) * 8 * NSUB)
                    W(sc, "bps", 16 * (j + 1))
                    sc.dma_start(out=P_sb.ap()[:, 0:TB],
                                 in_=p_scr[e:e + 1, poff:poff + TB].broadcast_to([P, TB])
                                 ).then_inc(sems["bpbc"], 16)
                    k = 0
                    for hcp in range(16):
                        for ts_ in range(NSUB):
                            U = (j * E + e) * 16 * NSUB + hcp * NSUB + ts_
                            W(sc, "bh1", U + 1)
                            W(sc, "bhid", U - 2)
                            I(sc.activation(s1t[U % 3][:], psH1[U % 3][:], AF.Silu), "bsil")
                            k += 1
                            if k % 4 == 0 and spend:
                                epi_silu(j - 1, spend.pop(0))
                    while spend:
                        epi_silu(j - 1, spend.pop(0))
            for m in range(NG):
                epi_silu(NB - 1, m)

        @blk.tensor
        def _(te):
            W(te, "aux", AUX_N)
            W(te, "w_att", 16 * (4 * NG + NG))
            def t_stats(ci):
                par = ci % 2
                W(te, "row", ci)
                W(te, "xnf_done", ci)
                W(te, "sev", 8 * ci - 8)
                W(te, "x_ld", 16 * 16 * (ci + 1))
                W(te, "sq", ci + 1)
                for g in range(NG):
                    te.matmul(psA[0:1, 0:CC], oc_sb[:], xbuf[par][:, g * CC:(g + 1) * CC],
                              start=(g == 0), stop=(g == NG - 1))
                for g in range(NG):
                    ins = te.matmul(psB[0:1, 0:CC], oc_sb[:], xnf_cur3(g),
                                    start=(g == 0), stop=(g == NG - 1))
                I(ins, "stat")

            def t_bcast(ci):
                W(te, "rstd", ci + 1)
                W(te, "row", ci + 1)
                te.matmul(psMb[:, 0:CC], or_sb[:], mrow, start=True, stop=True)
                I(te.matmul(psRb[:, 0:CC], or_sb[:], rstdr, start=True, stop=True), "bcast")

            t_stats(0)
            t_bcast(0)
            for ci in range(NCH):
                W(te, "mix", ci + 1)
                for m in range(NG):
                    G = ci * 8 + m
                    pg = G % 2
                    W(te, "d2s", G - 1)
                    for g in range(NG):
                        ins = te.matmul(psK[pg][:, 0:CC],
                                        wk_sb[:, g * D + m * P:g * D + (m + 1) * P],
                                        xk_sb[:, g * CC:(g + 1) * CC],
                                        start=(g == 0), stop=(g == NG - 1))
                    I(ins, "kv")
                    W(te, "kvcp", G - 1)
                    for g in range(NG):
                        ins = te.matmul(psV[pg][:, 0:CC],
                                        wv_sb[:, g * D + m * P:g * D + (m + 1) * P],
                                        xv_sb[:, g * CC:(g + 1) * CC],
                                        start=(g == 0), stop=(g == NG - 1))
                    I(ins, "kv")
                    W(te, "evr", G - 1)
                    for g in range(NG):
                        ins = te.matmul(psRr[pg][:, 0:CC],
                                        wr_sb[:, g * D + m * P:g * D + (m + 1) * P],
                                        xr_sb[:, g * CC:(g + 1) * CC],
                                        start=(g == 0), stop=(g == NG - 1))
                    I(ins, "kv")
                    if m == 3 and ci + 1 < NCH:
                        t_stats(ci + 1)
                        t_bcast(ci + 1)
                W(te, "grpdone", 8 * (ci + 1))
                for m in range(NG):
                    W(te, "sev", 8 * ci + m - 1)
                    for g in range(NG):
                        ins = te.matmul(psS[m % 2][:, 0:CC],
                                        wo_sb[:, g * D + m * P:g * D + (m + 1) * P],
                                        sry[:, g * CC:(g + 1) * CC],
                                        start=(g == 0), stop=(g == NG - 1))
                    I(ins, "wo")
            W(te, "sev", 8 * NCH)
            I(te.matmul(psA[0:1, 0:4], oc_sb[:], xbuf[0][:, 0:4], start=True, stop=True), "phA")
            W(te, "phA", PHA_TOTAL)
            W(te, "phAD", 16)
            nbat = (NTT + 3) // 4
            for j in range(NB):
                W(te, "bxn", 16 * 8 * (j + 1))
                W(te, "bgev", j * E * 8 * NSUB)
                if j > 0:
                    W(te, "bg4", j)
                for tt in range(NTT):
                    for g in range(NG):
                        ins = te.matmul(psSc[:, tt * E:(tt + 1) * E],
                                        xnb[:, g * TB + tt * P:g * TB + (tt + 1) * P],
                                        gw_sb[:, g * E:(g + 1) * E],
                                        start=(g == 0), stop=(g == NG - 1),
                                        skip_group_check=True)
                I(ins, "bg1")
                W(te, "bg4", j + 1)
                for bat in range(nbat):
                    W(te, "bgc", nbat * j + bat)
                    for q in range(min(4, NTT - bat * 4)):
                        tt = bat * 4 + q
                        ins = te.transpose(psPT[0:E, q * P:(q + 1) * P],
                                           Pn[:, tt * E:(tt + 1) * E], eye_sb[:])
                    I(ins, "bgt")
                for e in range(E):
                    if e == 0:
                        W(te, "bgc", nbat * (j + 1))
                    for hcp in range(16):
                        S = j * 128 + e * 16 + hcp
                        W(te, "bw12" + "abcde"[S % 5], 16 * (S // 5 + 1))
                        for ts_ in range(NSUB):
                            U = (j * E + e) * 16 * NSUB + hcp * NSUB + ts_
                            W(te, "bhid", U - 2)
                            for g in range(NG):
                                ins = te.matmul(psH1[U % 3][:],
                                                w12_sb[S % 5][:, g * P:(g + 1) * P],
                                                xnb[:, g * TB + ts_ * 512:g * TB + ts_ * 512 + 512],
                                                start=(g == 0), stop=(g == NG - 1))
                            I(ins, "bh1")
                            for g in range(NG):
                                ins = te.matmul(psH2[U % 3][:],
                                                w12_sb[S % 5][:, NG * P + g * P:NG * P + (g + 1) * P],
                                                xnb[:, g * TB + ts_ * 512:g * TB + ts_ * 512 + 512],
                                                start=(g == 0), stop=(g == NG - 1))
                            I(ins, "bh2")
                    W(te, "bw3", 64 * (j * E + e + 1))
                    W(te, "bhid", (j * E + e) * 16 * NSUB + 16 * NSUB)
                    if e == 0:
                        W(te, "bg2", j + 1)
                    for m in range(NG):
                        for ts_ in range(NSUB):
                            V = (j * E + e) * 8 * NSUB + m * NSUB + ts_
                            W(te, "bgev", V - 1)
                            for hc in range(NHC):
                                ins = te.matmul(psG[V % 2][:],
                                                w3_sb[:, hc * D + m * P:hc * D + (m + 1) * P],
                                                hid[:, hc * TB + ts_ * 512:hc * TB + ts_ * 512 + 512],
                                                start=(hc == 0), stop=(hc == NHC - 1))
                            I(ins, "bG")

        @blk.vector
        def _(v):
            def vwait():
                v.wait_ge(sems["vraw"], vraw_n[0])

            def vraw(ins):
                vraw_n[0] += 1
                return ins.then_inc(sems["vraw"], 1)

            v.memset(cp[:], -1e38)
            v.memset(cab[:], 0.0)
            for g in range(NG):
                v.memset(xnfv[:, g, :, 0:1], 0.0)
            W(v, "sc_init", 1)
            v.tensor_scalar(w8[:], w8t[:], -1.0, None, ALU.mult)

            def round1(ci, m):
                G = ci * 8 + m
                pg = G % 2
                kps = psK[pg]
                k3 = kps.ap()[:, 0:CC].rearrange("p (s t) -> p s t", s=2)
                W(v, "kv", c_kv(ci, m, 2))
                I(v.tensor_copy(vg[pg][:], psV[pg][:, 0:CC]), "kvcp")
                v.tensor_copy(pbuf[pg].ap()[:, 0:2 * SS].rearrange(
                    "p (q t) -> p q t", q=2)[:, :, 0:1], cp[:, 2 * m:2 * m + 2])
                for s in range(2):
                    ins = v.tensor_tensor_scan(
                        pbuf[pg][:, s * SS + 1:(s + 1) * SS],
                        w8[:, m:m + 1].broadcast_to([P, TCA]),
                        kps[:, s * TCA:(s + 1) * TCA],
                        cp[:, 2 * m + s:2 * m + s + 1],
                        ALU.add, ALU.max)
                    vraw(ins)
                vwait()
                pv = pbuf[pg].ap()[:, 0:2 * SS].rearrange("p (s t) -> p s t", s=2)
                p3 = pv[:, :, 1:SS]
                psh3 = pv[:, :, 0:TCA]
                v.tensor_copy(cp[:, 2 * m:2 * m + 2], pbuf[pg].ap()[:, 0:2 * SS].rearrange(
                    "p (q t) -> p q t", q=2)[:, :, TCA:TCA + 1])
                v.scalar_tensor_tensor(flat3(f1d[pg]), psh3, w8[:, m:m + 1],
                                       p3, ALU.add, ALU.subtract)
                v.scalar_tensor_tensor(flat3(tB2), k3, u8[:, m:m + 1],
                                       psh3, ALU.add, ALU.max)
                v.tensor_tensor(flat3(f2d[pg]), k3, p3, ALU.subtract)
                v.tensor_tensor(flat3(tmax[pg]), psh3, flat3(tB2), ALU.subtract)
                I(v.scalar_tensor_tensor(flat3(tmin[pg]), k3, u8[:, m:m + 1],
                                         flat3(tB2), ALU.add, ALU.subtract), "d2s")

            def round2a(ci, m2):
                G2 = ci * 8 + m2
                p2 = G2 % 2
                qb = qbuf[p2]
                q4 = qb.ap()[:, 0:4 * SS].rearrange("p (q t) -> p q t", q=4)
                W(v, "s2d", G2 + 1)
                if m2 == 0:
                    W(v, "wo", 8 * ci)
                v.tensor_tensor(f2vb[:], f2d[p2][:], vg[p2][:], ALU.mult)
                v.tensor_copy(q4[:, :, 0:1], cab[:, 4 * m2:4 * m2 + 4])
                for s in range(2):
                    v.tensor_tensor_scan(
                        qb.ap()[:, s * SS + 1:(s + 1) * SS],
                        f1d[p2][:, s * TCA:(s + 1) * TCA],
                        f2vb[:, s * TCA:(s + 1) * TCA],
                        cab[:, 4 * m2 + s:4 * m2 + s + 1],
                        ALU.mult, ALU.add)
                    ins = v.tensor_tensor_scan(
                        qb.ap()[:, (2 + s) * SS + 1:(3 + s) * SS],
                        f1d[p2][:, s * TCA:(s + 1) * TCA],
                        f2d[p2][:, s * TCA:(s + 1) * TCA],
                        cab[:, 4 * m2 + 2 + s:4 * m2 + 3 + s],
                        ALU.mult, ALU.add)
                    vraw(ins)
                vwait()
                I(v.tensor_copy(cab[:, 4 * m2:4 * m2 + 4], q4[:, :, TCA:TCA + 1]),
                  "vprod")

            def round2b(ci, m2):
                G2 = ci * 8 + m2
                p2 = G2 % 2
                W(v, "gpd", G2 + 1)
                v.reciprocal(tA2[:], f2d[p2][:])
                I(v.tensor_tensor(sry[:, m2 * CC:(m2 + 1) * CC], f1d[p2][:], tA2[:], ALU.mult),
                  "grpdone")

            def v_rows(ci):
                W(v, "stat", ci + 1)
                v.tensor_scalar(mrow, psA[0:1, 0:CC], 1.0 / D, None, ALU.mult)
                v.tensor_scalar(vrow, psB[0:1, 0:CC], 1.0 / D, None, ALU.mult)
                v.tensor_tensor(rstdr, mrow, mrow, ALU.mult)
                v.tensor_tensor(vrow, vrow, rstdr, ALU.subtract)
                I(v.tensor_scalar(vrow, vrow, 1e-5, None, ALU.add), "row")

            def v_xn(ci):
                par = ci % 2
                W(v, "bcast", ci + 1)
                W(v, "xnsp", 256 * ci)
                psM3 = psMb.ap()[:, 0:CC].rearrange("p (s t) -> p s t", s=2)
                psR3 = psRb.ap()[:, 0:CC].rearrange("p (s t) -> p s t", s=2)
                for g in range(NG):
                    v.tensor_tensor(flat3(t0buf), flat3(xbuf[par], g * CC), psM3, ALU.subtract)
                    ins = v.tensor_tensor(xnf_cur3(g), flat3(t0buf), psR3, ALU.mult)
                I(ins, "xnf_done")

            def v_mix(ci):
                W(v, "kv", 24 * ci)
                for g in range(NG):
                    v.tensor_tensor(flat3(dbuf), xnf_cur3(g), xnf_sh3(g), ALU.subtract)
                    v.scalar_tensor_tensor(flat3(xk_sb, g * CC), flat3(dbuf), tk8[:, g:g + 1],
                                           xnf_sh3(g), ALU.mult, ALU.add)
                    v.scalar_tensor_tensor(flat3(xv_sb, g * CC), flat3(dbuf), tv8[:, g:g + 1],
                                           xnf_sh3(g), ALU.mult, ALU.add)
                    v.scalar_tensor_tensor(flat3(xr_sb, g * CC), flat3(dbuf), tr8[:, g:g + 1],
                                           xnf_sh3(g), ALU.mult, ALU.add)
                for g in range(NG):
                    ins = v.tensor_copy(xnfv[:, g, :, 0:1], xnfv[:, g, :, TCA:TCA + 1])
                I(ins, "mix")

            v_rows(0)
            v_xn(0)
            v_mix(0)
            for ci in range(NCH):
                for m in range(NG):
                    if m >= 2:
                        round2b(ci, m - 2)
                    round1(ci, m)
                    if m >= 1:
                        round2a(ci, m - 1)
                    if ci + 1 < NCH:
                        if m == 3:
                            v_rows(ci + 1)
                        elif m == 4:
                            v_xn(ci + 1)
                        elif m == 5:
                            v_mix(ci + 1)
                round2b(ci, 6)
                round2a(ci, 7)
                round2b(ci, 7)
            I(v.memset(dbuf[:, 0:4], 0.0), "phA")
            W(v, "phA", PHA_TOTAL)
            W(v, "phAD", 16)
            nbat = (NTT + 3) // 4

            def epi_piece(j, m):
                idx = j * 8 + m
                W(v, "bxs", 32 * (idx + 1))
                W(v, "bsg", 8 * j + m + 1)
                v.tensor_tensor(obuf[:], gs_sb[:], sxs[idx % 2][:], ALU.mult)
                I(v.tensor_tensor(sxx[idx % 2][:], sxx[idx % 2][:], obuf[:], ALU.add),
                  "bepi")

            for j in range(NB):
                W(v, "bg1", j + 1)
                for tt in range(NTT):
                    v.reduce_max(m1n8[:, tt:tt + 1], psSc[:, tt * E:(tt + 1) * E],
                                 axis=AX.X, negate=True)
                for tt in range(NTT):
                    ins = v.tensor_scalar(scn[:, tt * E:(tt + 1) * E],
                                          psSc[:, tt * E:(tt + 1) * E],
                                          m1n8[:, tt:tt + 1], None, ALU.add)
                I(ins, "bg2")
                for tt in range(NTT):
                    v.tensor_scalar(sc2[:, tt * E:(tt + 1) * E], scn[:, tt * E:(tt + 1) * E],
                                    0.0, -1e30, ALU.is_ge, ALU.mult)
                for tt in range(NTT):
                    v.tensor_tensor(sc2[:, tt * E:(tt + 1) * E], scn[:, tt * E:(tt + 1) * E],
                                    sc2[:, tt * E:(tt + 1) * E], ALU.add)
                for tt in range(NTT):
                    v.reduce_max(m28[:, tt:tt + 1], sc2[:, tt * E:(tt + 1) * E], axis=AX.X)
                for tt in range(NTT):
                    v.tensor_scalar(mkb[:, tt * E:(tt + 1) * E], scn[:, tt * E:(tt + 1) * E],
                                    m28[:, tt:tt + 1], None, ALU.is_ge)
                W(v, "bg3", j + 1)
                for tt in range(NTT):
                    v.tensor_tensor(w8b[:, tt * E:(tt + 1) * E], ewb[:, tt * E:(tt + 1) * E],
                                    mkb[:, tt * E:(tt + 1) * E], ALU.mult)
                for tt in range(NTT):
                    ins = v.reduce_sum(z8[:, tt:tt + 1], w8b[:, tt * E:(tt + 1) * E], axis=AX.X)
                vraw(ins)
                vwait()
                vraw(v.reciprocal(rz8[:], z8[:]))
                vwait()
                for tt in range(NTT):
                    ins = v.tensor_scalar(Pn[:, tt * E:(tt + 1) * E], w8b[:, tt * E:(tt + 1) * E],
                                          rz8[:, tt:tt + 1], None, ALU.mult)
                I(ins, "bg4")
                W(v, "bps", 16 * j)
                for bat in range(nbat):
                    W(v, "bgt", nbat * j + bat + 1)
                    nq = min(4, NTT - bat * 4)
                    I(v.tensor_copy(PT_sb[:, bat * 512:bat * 512 + nq * P],
                                    psPT[0:E, 0:nq * P]), "bgc")
                pend = [m for m in range(NG)] if j > 0 else []
                for e in range(E):
                    k = 0
                    for hcp in range(16):
                        for ts_ in range(NSUB):
                            U = (j * E + e) * 16 * NSUB + hcp * NSUB + ts_
                            W(v, "bsil", U + 1)
                            W(v, "bh2", U + 1)
                            I(v.scalar_tensor_tensor(
                                hid[:, hcp * TB + ts_ * 512:hcp * TB + ts_ * 512 + 512],
                                s1t[U % 3][:], 1.0, psH2[U % 3][:], ALU.bypass, ALU.mult),
                              "bhid")
                            k += 1
                            if k % 4 == 0 and pend:
                                epi_piece(j - 1, pend.pop(0))
                    while pend:
                        epi_piece(j - 1, pend.pop(0))
                    W(v, "bpbc", 16 * (j * E + e + 1))
                    if e == 0:
                        W(v, "bsg", 8 * j)
                    for m in range(NG):
                        for ts_ in range(NSUB):
                            V = (j * E + e) * 8 * NSUB + m * NSUB + ts_
                            W(v, "bG", V + 1)
                            gsl = gacc[:, m * TB + ts_ * 512:m * TB + ts_ * 512 + 512]
                            if e == 0:
                                I(v.tensor_tensor(gsl, psG[V % 2][:],
                                                  P_sb[:, ts_ * 512:(ts_ + 1) * 512],
                                                  ALU.mult), "bgev")
                            else:
                                v.tensor_tensor(bt1[:], psG[V % 2][:],
                                                P_sb[:, ts_ * 512:(ts_ + 1) * 512], ALU.mult)
                                I(v.tensor_tensor(gsl, gsl, bt1[:], ALU.add), "bgev")
            for m in range(NG):
                epi_piece(NB - 1, m)

    nc.finalize()
    return nc


_NC_CACHE = {}


def _get_nc(T):
    if T not in _NC_CACHE:
        _NC_CACHE[T] = build(T=T)
    return _NC_CACHE[T]


def _prep(L0_attn_x, L0_attn_time_decay, L0_attn_time_first, L0_attn_time_mix_k,
          L0_attn_time_mix_v, L0_attn_time_mix_r, L0_attn_Wk, L0_attn_Wv,
          L0_attn_Wr, L0_attn_Wo, L0_ffn_w12, L0_ffn_w3, L0_ffn_gate_w):
    x = np.asarray(L0_attn_x, dtype=np.float32)
    B, T, C = x.shape
    xT = np.ascontiguousarray(x.transpose(0, 2, 1))
    w12 = np.asarray(L0_ffn_w12, np.float32)
    w3 = np.asarray(L0_ffn_w3, np.float32)
    shared = dict(
        wkT=np.ascontiguousarray(np.asarray(L0_attn_Wk, np.float32).T),
        wvT=np.ascontiguousarray(np.asarray(L0_attn_Wv, np.float32).T),
        wrT=np.ascontiguousarray(np.asarray(L0_attn_Wr, np.float32).T),
        woT=np.ascontiguousarray(np.asarray(L0_attn_Wo, np.float32).T),
        w12H=(lambda w12H_: np.ascontiguousarray(
            np.concatenate([w12H_[:, :H // P], w12H_[:, H // P:]], axis=3)))(
            w12.reshape(E, 2 * H // P, P, NG, P).transpose(0, 1, 4, 3, 2)
            .reshape(E, 2 * H // P, P, NG * P)),
        w3H=np.ascontiguousarray(
            w3.reshape(E, NG, P, H // P, P).transpose(0, 3, 4, 1, 2)
            .reshape(E, 4, H // P // 4, P, NG * P).transpose(0, 1, 3, 2, 4)
            .reshape(E, 4, P, (H // P // 4) * NG * P)),
        gwT=np.ascontiguousarray(np.asarray(L0_ffn_gate_w, np.float32).T),
        tdec=np.ascontiguousarray(np.asarray(L0_attn_time_decay, np.float32).reshape(D, 1)),
        tfir=np.ascontiguousarray(np.asarray(L0_attn_time_first, np.float32).reshape(D, 1)),
        tmk=np.ascontiguousarray(np.asarray(L0_attn_time_mix_k, np.float32).reshape(D, 1)),
        tmv=np.ascontiguousarray(np.asarray(L0_attn_time_mix_v, np.float32).reshape(D, 1)),
        tmr=np.ascontiguousarray(np.asarray(L0_attn_time_mix_r, np.float32).reshape(D, 1)),
        eyeI=np.eye(P, dtype=np.float32),
        onesC=np.ones((P, 1), np.float32),
        onesR=np.ones((1, P), np.float32),
    )
    return xT, shared, B, T


def kernel(**inputs):
    xT, shared, B, T = _prep(**inputs)
    n_cores = 8
    bpc = B // n_cores
    nc = _get_nc(T)
    in_maps = [dict(xT=np.ascontiguousarray(xT[c * bpc:(c + 1) * bpc]), **shared)
               for c in range(n_cores)]
    res = run_bass_kernel_spmd(nc, in_maps, core_ids=list(range(n_cores)))
    outT = np.concatenate([res.results[c]["outT"] for c in range(n_cores)], axis=0)
    return np.ascontiguousarray(outT.transpose(0, 2, 1))
